# version 04
# baseline (speedup 1.0000x reference)
import numpy as np
import ml_dtypes

import concourse.bass as bass
import concourse.library_config as libcfg
import concourse.mybir as mybir
from concourse.bass_core_rust import IndirectOffsetOnAxis
from concourse.bass_utils import run_bass_kernel_spmd

F32 = mybir.dt.float32
BF16 = mybir.dt.bfloat16
U32 = mybir.dt.uint32
I16 = mybir.dt.int16
I32 = mybir.dt.int32
OP = mybir.AluOpType
AX = mybir.AxisListType
ACT = mybir.ActivationFunctionType
AP = bass.AP

B, T, D = 4, 8192, 1024
HEADS, N = 8, 32
M = HEADS * N
E, KCAP, FF = 8, 512, 4096
W = 384
HALF = 4096
NTOK = HALF + W
L = 512
STEPS = L + W
NT128 = NTOK // 128
P1W = [512] * 8 + [384]
RWIN = 8
SWIN = 64
NRW = STEPS // RWIN
NSW = STEPS // SWIN
WARM_SW = W // SWIN
NREAL = NSW - WARM_SW
BISECT = 76
NP3 = HALF // 128
NP3W = HALF // 512
NFC = FF // 128
KB16 = KCAP // 16
NCORE = 8
EPS_LN = 1e-5
EPS_KN = 1e-6


def dram_ap(t, off, dims):
    return AP(t.tensor, off, [list(d) for d in dims])


class Sched:

    def __init__(self, tok=None):
        self.items = []
        self.tok = tok

    def on(self, eng_name, weight=1):
        def deco(fn):
            self.items.append((eng_name, fn, weight))
            return fn
        return deco

    def emit(self, eng_name, eng, tok):
        cum = 0
        for nm, fn, weight in self.items:
            if nm == eng_name:
                if cum > 0:
                    eng.wait_ge(tok, cum)
                last = fn(eng)
                if weight == 1:
                    if last is not None:
                        last.then_inc(tok, 1)
                    else:
                        eng.sem_inc(tok, 1)
            cum += weight


def build_L1(stage=2, parts=63):
    nc = bass.Bass("TRN2", target_bir_lowering=False, debug=False,
                   detect_race_conditions=False, enable_asserts=False)
    I = lambda n, s, d: nc.dram_tensor(n, s, d, kind="ExternalInput").ap()
    O = lambda n, s, d: nc.dram_tensor(n, s, d, kind="ExternalOutput").ap()
    x_pad = I("x_pad", [NTOK, D], F32)
    w_in = I("w_in", [D, D], F32)
    w_k = I("w_k", [M, D], F32)
    w_v = I("w_v", [M, D], F32)
    w_q = I("w_q", [M, D], F32)
    w_b = I("w_b", [M, D], F32)
    b_beta = I("b_beta_col", [128, 2], F32)
    ident_bf = I("ident_bf", [128, 128], BF16)
    ident_f = I("ident_f", [128, 128], F32)
    e4h = I("e4h", [128, 4], F32)
    e4h_t = I("e4h_t", [4, 128], F32)
    cbias = I("cbias", [128, 4], F32)

    kn_st = O("kn_st", [2, 4, NTOK, N], F32)
    q_st = O("q_st", [2, 4, NTOK, N], F32)
    b_st = O("b_st", [2, 128, NTOK], F32)
    v_st = O("v_st", [2, 128, NTOK], F32)
    xln_st = O("xln_st", [NTOK, D], BF16)

    sbt = nc.alloc_sbuf_tensor
    c_identbf = sbt("c_identbf", [128, 128], BF16)
    c_identf = sbt("c_identf", [128, 128], F32)
    c_e4h = sbt("c_e4h", [128, 4], F32)
    c_e4ht = sbt("c_e4ht", [4, 128], F32)
    c_bbeta = sbt("c_bbeta", [128, 2], F32)
    c_eps = sbt("c_eps", [128, 4], F32)
    wraw = sbt("wraw", [128, 8, 1024], BF16)
    w_inT = sbt("w_inT", [128, 64, 128], BF16)
    wT = {"k": sbt("w_kT", [128, 16, 128], BF16),
          "v": sbt("w_vT", [128, 16, 128], BF16),
          "q": sbt("w_qT", [128, 16, 128], BF16),
          "b": sbt("w_bT", [128, 16, 128], BF16)}
    xt = sbt("xt", [128, 2, D], F32)
    xsq = sbt("xsq", [128, D], F32)
    st8 = sbt("st8", [128, 8], F32)
    xlnb = sbt("xlnb", [128, 2, D], BF16)
    xfm = sbt("xfm", [128, 2, 8, 512], BF16)
    xpb = sbt("xpb", [128, 8, 512], BF16)
    pj = sbt("pj", [128, 4, 512], F32)
    knbuf = sbt("knbuf", [128, 512], F32)
    ntmp = sbt("ntmp", [4, 512], F32)
    ntmp2 = sbt("ntmp2", [4, 512], F32)
    ttile = sbt("ttile", [128, 128], F32)
    pbank = [nc.alloc_psum_tensor(f"pb{i}", [128, 512], F32) for i in range(8)]

    S = Sched()
    dc1 = {"n": 0}
    dc2 = {"n": 0}
    dc3 = {"n": 0}

    with (
        nc.semaphore("tok") as tok, nc.semaphore("sd1") as sd1,
        nc.semaphore("sd2") as sd2, nc.semaphore("sd3") as sd3,
        nc.Block() as block,
    ):
        def dma1(sy, out, in_):
            sy.dma_start(out=out, in_=in_).then_inc(sd1, 16)
            dc1["n"] += 16

        def dma1t(sy, out, in_):
            sy.dma_start_transpose(out=out, in_=in_).then_inc(sd1, 16)
            dc1["n"] += 16

        def dma2(g, out, in_):
            g.dma_start(out=out, in_=in_).then_inc(sd2, 16)
            dc2["n"] += 16

        def dma3(sc, out, in_):
            sc.dma_start(out=out, in_=in_).then_inc(sd3, 16)
            dc3["n"] += 16

        def w1(sy):
            sy.wait_ge(sd1, dc1["n"])

        def w2(g):
            g.wait_ge(sd2, dc2["n"])

        def w3(sc):
            sc.wait_ge(sd3, dc3["n"])

        @S.on("sync")
        def _(sy):
            for dst, src in [(c_identbf, ident_bf), (c_identf, ident_f),
                             (c_e4h, e4h), (c_e4ht, e4h_t), (c_bbeta, b_beta),
                             (c_eps, cbias)]:
                dma1(sy, dst[:], src)
            w1(sy)

        @S.on("vector")
        def _(ve):
            return ve.memset(st8[:], 0)
        @S.on("scalar")
        def _(sc):
            sc.square(out=st8[:, 0:1], in_=st8[:, 1:2])
            sc.activation(out=st8[:, 0:1], in_=st8[:, 1:2], func=ACT.Sqrt)
            sc.activation(out=st8[:, 0:1], in_=st8[:, 1:2], func=ACT.Identity)
            sc.activation(out=st8[:, 0:1], in_=st8[:, 1:2], func=ACT.Sigmoid)
            sc.activation(out=st8[:, 0:1], in_=st8[:, 1:2], func=ACT.Silu)
            return sc.activation(out=st8[:, 0:1], in_=st8[:, 1:2], func=ACT.Tanh)
        @S.on("gpsimd")
        def _(g):
            for _i in range(4):
                g.memset(wraw[:, :, :], 0)
            return g.memset(wraw[:, :, :], 0)

        WLIST = [(w_in, 8, w_inT, 64), (w_k, 2, wT["k"], 16),
                 (w_v, 2, wT["v"], 16), (w_q, 2, wT["q"], 16),
                 (w_b, 2, wT["b"], 16)]
        for (wsrc, nrc, dstT, ntile) in WLIST:
            @S.on("gpsimd")
            def _(g, wsrc=wsrc, nrc=nrc):
                src_ap = dram_ap(wsrc, 0, [(D, 128), (128 * D, nrc), (1, D)])
                dma2(g, wraw[:, 0:nrc, :], src_ap)
                w2(g)
            for cc in range(8):
                for rc in range(nrc):
                    @S.on("tensor")
                    def _(te, rc=rc, cc=cc):
                        return te.transpose(
                            out=pbank[6].bitcast(BF16)[:, 0:128],
                            in_=wraw[:, rc, cc * 128:(cc + 1) * 128],
                            identity=c_identbf[:])
                    @S.on("vector")
                    def _(ve, rc=rc, cc=cc, dstT=dstT, nrc=nrc):
                        return ve.tensor_copy(
                            out=dstT[:, cc * nrc + rc, :],
                            in_=pbank[6].bitcast(BF16)[:, 0:128])

        if stage < 1:
            _finish(S, w1, w2, w3)
            _emit_all(S, block, tok)
            return nc

        for i in range(NT128):
            @S.on("sync")
            def _(sy, i=i):
                dma1(sy, xt[:, i % 2], x_pad[i * 128:(i + 1) * 128, :])
                w1(sy)
            @S.on("vector")
            def _(ve, i=i):
                x = xt[:, i % 2]
                return ve.reduce_sum(out=st8[:, 0:1], in_=x, axis=AX.X)
            @S.on("scalar")
            def _(sc, i=i):
                return sc.square(out=xsq[:], in_=xt[:, i % 2])
            @S.on("vector")
            def _(ve, i=i):
                ve.reduce_sum(out=st8[:, 1:2], in_=xsq[:], axis=AX.X)
                ve.tensor_scalar_mul(st8[:, 2:3], st8[:, 0:1], 1.0 / D)
                ve.tensor_scalar(out=st8[:, 3:4], in0=st8[:, 2:3],
                                 scalar1=st8[:, 2:3], scalar2=None, op0=OP.mult)
                return ve.scalar_tensor_tensor(
                    out=st8[:, 4:5], in0=st8[:, 1:2], scalar=1.0 / D,
                    in1=st8[:, 3:4], op0=OP.mult, op1=OP.subtract)
            @S.on("scalar")
            def _(sc, i=i):
                return sc.activation(out=st8[:, 5:6], in_=st8[:, 4:5],
                                     func=ACT.Sqrt, bias=c_eps[:, 0:1])
            @S.on("vector")
            def _(ve, i=i):
                return ve.reciprocal(out=st8[:, 6:7], in_=st8[:, 5:6])
            @S.on("vector")
            def _(ve, i=i):
                return ve.tensor_scalar(out=st8[:, 7:8], in0=st8[:, 2:3],
                                        scalar1=st8[:, 6:7], scalar2=-1.0,
                                        op0=OP.mult, op1=OP.mult)
            @S.on("scalar")
            def _(sc, i=i):
                return sc.activation(out=xlnb[:, i % 2], in_=xt[:, i % 2],
                                     func=ACT.Identity, scale=st8[:, 6:7],
                                     bias=st8[:, 7:8])
            @S.on("sync")
            def _(sy, i=i):
                dma1(sy, xln_st[i * 128:(i + 1) * 128, :], xlnb[:, i % 2])
                w1(sy)

        if stage < 2:
            _finish(S, w1, w2, w3)
            _emit_all(S, block, tok)
            return nc

        t0 = 0
        for w in range(9):
            nw = P1W[w]
            ns = nw // 128
            @S.on("sync")
            def _(sy, w=w, nw=nw, t0=t0):
                for dc in range(8):
                    dma1t(sy, xfm[:, w % 2, dc, 0:nw],
                          xln_st[t0:t0 + nw, dc * 128:(dc + 1) * 128])
                w1(sy)
            for ec in range(8 if parts & 2 else 0):
                @S.on("tensor", weight=8)
                def _(te, w=w, nw=nw, ec=ec):
                    for dc in range(8):
                        te.matmul(out=pbank[ec % 2][:, 0:nw],
                                  lhsT=w_inT[:, dc * 8 + ec, :],
                                  rhs=xfm[:, w % 2, dc, 0:nw],
                                  start=(dc == 0), stop=(dc == 7)
                                  ).then_inc(tok, 1)
                @S.on("scalar")
                def _(sc, w=w, nw=nw, ec=ec):
                    return sc.activation(out=xpb[:, ec, 0:nw],
                                         in_=pbank[ec % 2][:, 0:nw],
                                         func=ACT.Silu)
            for hg in range(2 if parts & 4 else 0):
                for pi, pn in enumerate(["k", "v", "q", "b"]):
                    @S.on("tensor", weight=8)
                    def _(te, w=w, nw=nw, hg=hg, pi=pi, pn=pn):
                        for ec in range(8):
                            te.matmul(out=pbank[2 + pi][:, 0:nw],
                                      lhsT=wT[pn][:, ec * 2 + hg, :],
                                      rhs=xpb[:, ec, 0:nw],
                                      start=(ec == 0), stop=(ec == 7)
                                      ).then_inc(tok, 1)
                if not parts & 8:
                    continue
                @S.on("scalar")
                def _(sc, w=w, nw=nw, hg=hg, t0=t0):
                    sc.activation(out=pj[:, 3, 0:nw], in_=pbank[5][:, 0:nw],
                                  func=ACT.Sigmoid, bias=c_bbeta[:, hg:hg + 1])
                    sc.copy(out=pj[:, 1, 0:nw], in_=pbank[3][:, 0:nw])
                    sc.copy(out=pj[:, 0, 0:nw], in_=pbank[2][:, 0:nw])
                    return sc.square(out=pj[:, 2, 0:nw], in_=pbank[2][:, 0:nw])
                @S.on("gpsimd")
                def _(g, w=w, nw=nw, hg=hg, t0=t0):
                    dma2(g, dram_ap(b_st, hg * 128 * NTOK + t0,
                                    [(NTOK, 128), (1, nw)]), pj[:, 3, 0:nw])
                    dma2(g, dram_ap(v_st, hg * 128 * NTOK + t0,
                                    [(NTOK, 128), (1, nw)]), pj[:, 1, 0:nw])
                    w2(g)
                if not parts & 16:
                    continue
                @S.on("tensor")
                def _(te, w=w, nw=nw, hg=hg):
                    return te.matmul(out=pbank[6][0:4, 0:nw], lhsT=c_e4h[:],
                                     rhs=pj[:, 2, 0:nw], start=True, stop=True)
                @S.on("scalar")
                def _(sc, w=w, nw=nw, hg=hg):
                    return sc.activation(out=ntmp[:, 0:nw],
                                         in_=pbank[6][0:4, 0:nw], func=ACT.Sqrt)
                @S.on("vector")
                def _(ve, w=w, nw=nw, hg=hg):
                    return ve.tensor_scalar_add(ntmp2[:, 0:nw], ntmp[:, 0:nw],
                                                EPS_KN)
                @S.on("vector")
                def _(ve, w=w, nw=nw, hg=hg):
                    return ve.reciprocal(out=ntmp[:, 0:nw], in_=ntmp2[:, 0:nw])
                @S.on("tensor")
                def _(te, w=w, nw=nw, hg=hg):
                    return te.matmul(out=pbank[7][:, 0:nw], lhsT=c_e4ht[:],
                                     rhs=ntmp[:, 0:nw], start=True, stop=True)
                @S.on("vector")
                def _(ve, w=w, nw=nw, hg=hg):
                    return ve.tensor_tensor(out=knbuf[:, 0:nw],
                                            in0=pj[:, 0, 0:nw],
                                            in1=pbank[7][:, 0:nw], op=OP.mult)
                if not parts & 32:
                    continue
                for j in range(ns):
                    @S.on("tensor")
                    def _(te, j=j, hg=hg):
                        return te.transpose(out=pbank[6][:, 0:128],
                                            in_=knbuf[:, j * 128:(j + 1) * 128],
                                            identity=c_identf[:])
                    @S.on("vector")
                    def _(ve, j=j):
                        return ve.tensor_copy(out=ttile[:], in_=pbank[6][:, 0:128])
                    @S.on("gpsimd")
                    def _(g, j=j, hg=hg, t0=t0):
                        dma2(g, dram_ap(kn_st, hg * 4 * NTOK * N + (t0 + j * 128) * N,
                                        [(N, 128), (NTOK * N, 4), (1, N)]),
                             ttile[:])
                        w2(g)
                @S.on("scalar")
                def _(sc, w=w, nw=nw):
                    return sc.copy(out=pj[:, 2, 0:nw], in_=pbank[4][:, 0:nw])
                for j in range(ns):
                    @S.on("tensor")
                    def _(te, j=j):
                        return te.transpose(out=pbank[6][:, 0:128],
                                            in_=pj[:, 2, j * 128:(j + 1) * 128],
                                            identity=c_identf[:])
                    @S.on("vector")
                    def _(ve, j=j):
                        return ve.tensor_copy(out=ttile[:], in_=pbank[6][:, 0:128])
                    @S.on("gpsimd")
                    def _(g, j=j, hg=hg, t0=t0):
                        dma2(g, dram_ap(q_st, hg * 4 * NTOK * N + (t0 + j * 128) * N,
                                        [(N, 128), (NTOK * N, 4), (1, N)]),
                             ttile[:])
                        w2(g)
            t0 += nw

        _finish(S, w1, w2, w3)
        _emit_all(S, block, tok)
    return nc


def _finish(S, w1, w2, w3):
    @S.on("scalar")
    def _(sc):
        w3(sc)
    @S.on("gpsimd")
    def _(g):
        w2(g)
    @S.on("sync")
    def _(sy):
        w1(sy)


def _emit_all(S, block, tok):
    @block.sync
    def _(sy):
        S.emit("sync", sy, tok)

    @block.gpsimd
    def _(g):
        S.emit("gpsimd", g, tok)

    @block.tensor
    def _(te):
        S.emit("tensor", te, tok)

    @block.vector
    def _(ve):
        S.emit("vector", ve, tok)

    @block.scalar
    def _(sc):
        S.emit("scalar", sc, tok)




def build_L2():
    nc = bass.Bass("TRN2", target_bir_lowering=False, debug=False,
                   detect_race_conditions=False, enable_asserts=False)
    I = lambda n, s, d: nc.dram_tensor(n, s, d, kind="ExternalInput").ap()
    kn_st = I("kn_st", [2, 4, NTOK, N], F32)
    q_st = I("q_st", [2, 4, NTOK, N], F32)
    b_st = I("b_st", [2, 128, NTOK], F32)
    v_st = I("v_st", [2, 128, NTOK], F32)
    outs_st = nc.dram_tensor("outs_st", [2, 128, HALF], F32,
                             kind="ExternalOutput").ap()

    sbt = nc.alloc_sbuf_tensor
    Sbuf = sbt("Sbuf", [128, 2, 8, 32], F32)
    tmpb = sbt("tmpb", [128, 2, 8, 32], F32)
    m1b = sbt("m1b", [128, 2, 8, 32], F32)
    m2b = sbt("m2b", [128, 2, 8, 32], F32)
    Abuf = sbt("Abuf", [128, 2, 8, 32], F32)
    tqb = sbt("tqb", [128, 2, 8, 32], F32)
    rbuf = sbt("rbuf", [128, 2, 8, 1], F32)
    dlbuf = sbt("dlbuf", [128, 2, 8, 1], F32)
    knr = sbt("knr", [128, 2, 2, 8, RWIN, N], F32)
    qr = sbt("qr", [128, 2, 2, 8, RWIN, N], F32)
    brr = sbt("brr", [128, 2, 2, 8, SWIN], F32)
    vrr = sbt("vrr", [128, 2, 2, 8, SWIN], F32)
    sqw = sbt("sqw", [128, 2, 2, 8, SWIN], F32)
    sgw = sbt("sgw", [128, 2, 8, SWIN], F32)
    bsc = sbt("bsc", [128, 2 * 8 * SWIN], F32)

    S = Sched()

    KNROW = 2 * 2 * 8 * RWIN * N

    def kn_src(stage, hg, h4, tau0):
        base = hg * (4 * NTOK * N) + h4 * (NTOK * N) + tau0 * N
        return dram_ap(stage, base,
                       [(0, 32), (L * N, 8), (1, RWIN * N)])

    def kn_dst(ring, k, hg, h4):
        return ring[h4 * 32:(h4 + 1) * 32, k, hg]

    def bv_src(stage, hg, m):
        base = hg * (128 * NTOK) + m * SWIN
        return dram_ap(stage, base, [(NTOK, 128), (L, 8), (1, SWIN)])

    def outs_dst(hg, mr):
        base = hg * (128 * HALF) + mr * SWIN
        return dram_ap(outs_st, base, [(HALF, 128), (L, 8), (1, SWIN)])

    with (
        nc.semaphore("tok") as tok, nc.semaphore("skq") as skq,
        nc.semaphore("sq2") as sq2,
        nc.semaphore("sbv") as sbv, nc.semaphore("sot") as sot,
        nc.Block() as block,
    ):
        @S.on("vector")
        def _(ve):
            return ve.memset(Sbuf[:], 0)

        @S.on("sync")
        def _(sy):
            for k in range(2):
                for hg in range(2):
                    for h4 in range(4):
                        sy.dma_start(out=kn_dst(knr, k, hg, h4),
                                     in_=kn_src(kn_st, hg, h4, k * RWIN)
                                     ).then_inc(skq, 16)
        @S.on("gpsimd")
        def _(g):
            for k in range(2):
                for hg in range(2):
                    for h4 in range(4):
                        g.dma_start(out=kn_dst(qr, k, hg, h4),
                                    in_=kn_src(q_st, hg, h4, k * RWIN)
                                    ).then_inc(sq2, 16)

        @S.on("gpsimd")
        def _(g):
            for m in range(2):
                for hg in range(2):
                    g.dma_start(out=brr[:, m, hg],
                                in_=bv_src(b_st, hg, m)).then_inc(sbv, 16)
                    g.dma_start(out=vrr[:, m, hg],
                                in_=bv_src(v_st, hg, m)).then_inc(sbv, 16)

        nkq_issued = [2]
        nbv_issued = [2]
        for tau in range(STEPS):
            kwin = tau // RWIN
            mwin = tau // SWIN
            real = tau >= W
            mr = (tau - W) // SWIN if real else -1

            if tau % RWIN == 0 and tau > 0 and nkq_issued[0] < NRW:
                k = nkq_issued[0]
                nkq_issued[0] += 1
                @S.on("sync")
                def _(sy, k=k):
                    for hg in range(2):
                        for h4 in range(4):
                            sy.dma_start(out=kn_dst(knr, k % 2, hg, h4),
                                         in_=kn_src(kn_st, hg, h4, k * RWIN)
                                         ).then_inc(skq, 16)
                @S.on("gpsimd")
                def _(g, k=k):
                    for hg in range(2):
                        for h4 in range(4):
                            g.dma_start(out=kn_dst(qr, k % 2, hg, h4),
                                        in_=kn_src(q_st, hg, h4, k * RWIN)
                                        ).then_inc(sq2, 16)
            if tau % SWIN == 0 and tau > 0 and nbv_issued[0] < NSW:
                m = nbv_issued[0]
                nbv_issued[0] += 1
                @S.on("gpsimd")
                def _(g, m=m):
                    for hg in range(2):
                        g.dma_start(out=brr[:, m % 2, hg],
                                    in_=bv_src(b_st, hg, m)).then_inc(sbv, 16)
                        g.dma_start(out=vrr[:, m % 2, hg],
                                    in_=bv_src(v_st, hg, m)).then_inc(sbv, 16)

            @S.on("vector")
            def _(ve, tau=tau, kwin=kwin, mwin=mwin, real=real, mr=mr):
                if tau % RWIN == 0:
                    ve.wait_ge(skq, 128 * (kwin + 1))
                    ve.wait_ge(sq2, 128 * (kwin + 1))
                if tau % SWIN == 0:
                    ve.wait_ge(sbv, 64 * (mwin + 1))
                if real and tau % SWIN == 0 and mr >= 2:
                    ve.wait_ge(sot, 32 * (mr - 1))
                kb = knr[:, kwin % 2, :, :, tau % RWIN, :]
                qb = qr[:, kwin % 2, :, :, tau % RWIN, :]
                bb = brr[:, mwin % 2, :, :, tau % SWIN:tau % SWIN + 1]
                vb = vrr[:, mwin % 2, :, :, tau % SWIN:tau % SWIN + 1]
                ve.tensor_tensor(out=tmpb[:], in0=Sbuf[:], in1=kb, op=OP.mult)
                ve.tensor_reduce(out=rbuf[:], in_=tmpb[:], axis=AX.X, op=OP.add)
                ve.drain()
                ve.scalar_tensor_tensor(out=dlbuf[:], in0=rbuf[:], scalar=-1.0,
                                        in1=vb, op0=OP.mult, op1=OP.add)
                ve.tensor_tensor(out=m1b[:], in0=bb.to_broadcast([128, 2, 8, N]),
                                 in1=Sbuf[:], op=OP.mult)
                ve.drain()
                ve.tensor_tensor(out=m2b[:],
                                 in0=dlbuf[:].to_broadcast([128, 2, 8, N]),
                                 in1=kb, op=OP.mult)
                ve.drain()
                return ve.tensor_tensor(out=Abuf[:], in0=m1b[:], in1=m2b[:],
                                         op=OP.add)

            @S.on("scalar")
            def _(sc, tau=tau):
                return sc.activation(out=Sbuf[:], in_=Abuf[:], func=ACT.Tanh)

            if real:
                @S.on("vector")
                def _(ve, tau=tau, kwin=kwin, mr=mr):
                    qb = qr[:, kwin % 2, :, :, tau % RWIN, :]
                    ve.tensor_tensor(out=tqb[:], in0=Sbuf[:], in1=qb, op=OP.mult)
                    return ve.tensor_reduce(
                        out=sqw[:, mr % 2, :, :, tau % SWIN:tau % SWIN + 1],
                        in_=tqb[:], axis=AX.X, op=OP.add)

            if real and tau % SWIN == SWIN - 1:
                @S.on("scalar")
                def _(sc, mr=mr):
                    return sc.activation(
                        out=sgw[:].rearrange("p a b c -> p (a b c)"),
                        in_=sqw[:, mr % 2].rearrange("p a b c -> p (a b c)"),
                        func=ACT.Sigmoid)
                @S.on("vector")
                def _(ve, mr=mr):
                    sq = sqw[:, mr % 2].rearrange("p a b c -> p (a b c)")
                    sg = sgw[:].rearrange("p a b c -> p (a b c)")
                    ve.tensor_tensor(out=bsc[:], in0=sq, in1=sq, op=OP.mult)
                    ve.drain()
                    return ve.tensor_tensor(out=sq, in0=bsc[:], in1=sg, op=OP.mult)
                @S.on("gpsimd")
                def _(g, mr=mr):
                    for hg in range(2):
                        g.dma_start(out=outs_dst(hg, mr),
                                    in_=sqw[:, mr % 2, hg]).then_inc(sot, 16)

        @S.on("gpsimd")
        def _(g):
            g.wait_ge(sot, 32 * NREAL)

        _emit_all(S, block, tok)
    return nc




def build_L3():
    nc = bass.Bass("TRN2", target_bir_lowering=False, debug=False,
                   detect_race_conditions=False, enable_asserts=False)
    I = lambda n, s, d: nc.dram_tensor(n, s, d, kind="ExternalInput").ap()
    O = lambda n, s, d: nc.dram_tensor(n, s, d, kind="ExternalOutput").ap()
    outs_st = I("outs_st", [2, 128, HALF], F32)
    x_pad = I("x_pad", [NTOK, D], F32)
    w_out = I("w_out", [D, M], F32)
    wg = I("wg", [E, D], F32)
    ident_bf = I("ident_bf", [128, 128], BF16)
    ident_f = I("ident_f", [128, 128], F32)
    ones8 = I("ones8", [8, 1], F32)
    ones1x8 = I("ones1x8", [1, 8], F32)
    cbias = I("cbias", [128, 4], F32)

    x1_out = O("x1_out", [HALF, D], F32)
    xln2_st = O("xln2_st", [HALF, D], BF16)
    sm_tm = O("sm_tm", [HALF, 64], F32)
    sm_half = O("sm_half", [8, HALF], F32)

    sbt = nc.alloc_sbuf_tensor
    c_identbf = sbt("c_identbf", [128, 128], BF16)
    c_identf = sbt("c_identf", [128, 128], F32)
    c_ones8 = sbt("c_ones8", [8, 1], F32)
    c_ones1x8 = sbt("c_ones1x8", [1, 8], F32)
    c_eps = sbt("c_eps", [128, 4], F32)
    wraw = sbt("wraw", [128, 8, 256], BF16)
    w_outT = sbt("w_outT", [128, 2, 1024], BF16)
    wgraw = sbt("wgraw", [8, 1024], BF16)
    wgT = sbt("wgT", [128, 8, 8], BF16)
    cellb = sbt("cellb", [128, 2, 128], BF16)
    xt3 = sbt("xt3", [128, D], F32)
    x1t = sbt("x1t", [128, D], F32)
    x3sq = sbt("x3sq", [128, D], F32)
    st8 = sbt("st8", [128, 8], F32)
    xln2b = sbt("xln2b", [128, D], BF16)
    x1fm = sbt("x1fm", [128, 8, 512], BF16)
    expvb = sbt("expvb", [8, 512], F32)
    rcpb = sbt("rcpb", [1, 512], F32)
    smwb = sbt("smwb", [8, HALF], F32)
    smtt = sbt("smtt", [128, 64], F32)
    pbank = [nc.alloc_psum_tensor(f"pb{i}", [128, 512], F32) for i in range(8)]

    S = Sched()
    dc1 = {"n": 0}
    dc2 = {"n": 0}

    with (
        nc.semaphore("tok") as tok, nc.semaphore("sd1") as sd1,
        nc.semaphore("sd2") as sd2,
        nc.Block() as block,
    ):
        def dma1(sy, out, in_):
            sy.dma_start(out=out, in_=in_).then_inc(sd1, 16)
            dc1["n"] += 16

        def dma1t(sy, out, in_):
            sy.dma_start_transpose(out=out, in_=in_).then_inc(sd1, 16)
            dc1["n"] += 16

        def dma2(g, out, in_):
            g.dma_start(out=out, in_=in_).then_inc(sd2, 16)
            dc2["n"] += 16

        def w1(sy):
            sy.wait_ge(sd1, dc1["n"])

        def w2(g):
            g.wait_ge(sd2, dc2["n"])

        @S.on("sync")
        def _(sy):
            for dst, src in [(c_identbf, ident_bf), (c_identf, ident_f),
                             (c_ones8, ones8),
                             (c_ones1x8, ones1x8), (c_eps, cbias)]:
                dma1(sy, dst[:], src)
            w1(sy)

        @S.on("vector")
        def _(ve):
            ve.memset(st8[:], 0)
            return ve.memset(smtt[:], 0)

        @S.on("scalar")
        def _(sc):
            sc.square(out=st8[:, 0:1], in_=st8[:, 1:2])
            sc.activation(out=st8[:, 0:1], in_=st8[:, 1:2], func=ACT.Sqrt)
            sc.activation(out=st8[:, 0:1], in_=st8[:, 1:2], func=ACT.Identity)
            return sc.activation(out=st8[:, 0:1], in_=st8[:, 1:2], func=ACT.Exp)

        @S.on("gpsimd")
        def _(g):
            for _i in range(5):
                g.memset(wraw[:, :, :], 0)
            src_ap = dram_ap(w_out, 0, [(M, 128), (128 * M, 8), (1, M)])
            dma2(g, wraw[:, :, :], src_ap)
            dma2(g, wgraw[:], dram_ap(wg, 0, [(D, 8), (1, D)]))
            w2(g)

        for dc in range(8):
            for mc in range(2):
                @S.on("tensor")
                def _(te, dc=dc, mc=mc):
                    return te.transpose(
                        out=pbank[6].bitcast(BF16)[:, 0:128],
                        in_=wraw[:, dc, mc * 128:(mc + 1) * 128],
                        identity=c_identbf[:])
                @S.on("vector")
                def _(ve, dc=dc, mc=mc):
                    return ve.tensor_copy(out=w_outT[:, mc, dc * 128:(dc + 1) * 128],
                                          in_=pbank[6].bitcast(BF16)[:, 0:128])
        for dc in range(8):
            @S.on("tensor")
            def _(te, dc=dc):
                return te.transpose(out=pbank[6].bitcast(BF16)[:, 0:8],
                                    in_=wgraw[:, dc * 128:(dc + 1) * 128],
                                    identity=c_identbf[0:8, 0:8])
            @S.on("vector")
            def _(ve, dc=dc):
                return ve.tensor_copy(out=wgT[:, dc, :],
                                      in_=pbank[6].bitcast(BF16)[:, 0:8])

        for it in range(NP3):
            @S.on("sync")
            def _(sy, it=it):
                dma1(sy, xt3[:], x_pad[W + it * 128:W + (it + 1) * 128, :])
                w1(sy)
            @S.on("gpsimd")
            def _(g, it=it):
                for hg in range(2):
                    g.dma_start(out=cellb[:, hg],
                                in_=dram_ap(outs_st, hg * 128 * HALF + it * 128,
                                            [(HALF, 128), (1, 128)])
                                ).then_inc(sd2, 16)
                    dc2["n"] += 16
                w2(g)
            @S.on("tensor", weight=4)
            def _(te, it=it):
                for dh in range(2):
                    te.matmul(out=pbank[dh][:], lhsT=cellb[:, 0],
                              rhs=w_outT[:, 0, dh * 512:(dh + 1) * 512],
                              start=True, stop=False).then_inc(tok, 1)
                    te.matmul(out=pbank[dh][:], lhsT=cellb[:, 1],
                              rhs=w_outT[:, 1, dh * 512:(dh + 1) * 512],
                              start=False, stop=True).then_inc(tok, 1)
            @S.on("vector")
            def _(ve, it=it):
                for dh in range(2):
                    ve.tensor_tensor(out=x1t[:, dh * 512:(dh + 1) * 512],
                                     in0=xt3[:, dh * 512:(dh + 1) * 512],
                                     in1=pbank[dh][:], op=OP.add)
                ve.drain()
                return ve.reduce_sum(out=st8[:, 0:1], in_=x1t[:], axis=AX.X)
            @S.on("scalar")
            def _(sc, it=it):
                return sc.square(out=x3sq[:], in_=x1t[:])
            @S.on("vector")
            def _(ve, it=it):
                ve.reduce_sum(out=st8[:, 1:2], in_=x3sq[:], axis=AX.X)
                ve.tensor_scalar_mul(st8[:, 2:3], st8[:, 0:1], 1.0 / D)
                ve.drain()
                ve.tensor_scalar(out=st8[:, 3:4], in0=st8[:, 2:3],
                                 scalar1=st8[:, 2:3], scalar2=None, op0=OP.mult)
                ve.drain()
                return ve.scalar_tensor_tensor(
                    out=st8[:, 4:5], in0=st8[:, 1:2], scalar=1.0 / D,
                    in1=st8[:, 3:4], op0=OP.mult, op1=OP.subtract)
            @S.on("scalar")
            def _(sc, it=it):
                return sc.activation(out=st8[:, 5:6], in_=st8[:, 4:5],
                                     func=ACT.Sqrt, bias=c_eps[:, 0:1])
            @S.on("vector")
            def _(ve, it=it):
                return ve.reciprocal(out=st8[:, 6:7], in_=st8[:, 5:6])
            @S.on("vector")
            def _(ve, it=it):
                return ve.tensor_scalar(out=st8[:, 7:8], in0=st8[:, 2:3],
                                        scalar1=st8[:, 6:7], scalar2=-1.0,
                                        op0=OP.mult, op1=OP.mult)
            @S.on("scalar")
            def _(sc, it=it):
                return sc.activation(out=xln2b[:], in_=x1t[:], func=ACT.Identity,
                                     scale=st8[:, 6:7], bias=st8[:, 7:8])
            @S.on("sync")
            def _(sy, it=it):
                dma1(sy, x1_out[it * 128:(it + 1) * 128, :], x1t[:])
                w1(sy)
            @S.on("gpsimd")
            def _(g, it=it):
                dma2(g, xln2_st[it * 128:(it + 1) * 128, :], xln2b[:])
                w2(g)

        for w in range(NP3W):
            @S.on("sync")
            def _(sy, w=w):
                for dc in range(8):
                    dma1t(sy, x1fm[:, dc, :],
                          xln2_st[w * 512:(w + 1) * 512, dc * 128:(dc + 1) * 128])
                w1(sy)
            @S.on("tensor", weight=8)
            def _(te, w=w):
                for dc in range(8):
                    te.matmul(out=pbank[2][0:8, :], lhsT=wgT[:, dc, :],
                              rhs=x1fm[:, dc, :], start=(dc == 0),
                              stop=(dc == 7)).then_inc(tok, 1)
            @S.on("scalar")
            def _(sc, w=w):
                return sc.activation(out=expvb[:], in_=pbank[2][0:8, :],
                                     func=ACT.Exp)
            @S.on("tensor")
            def _(te, w=w):
                return te.matmul(out=pbank[3][0:1, :], lhsT=c_ones8[:],
                                 rhs=expvb[:], start=True, stop=True)
            @S.on("vector")
            def _(ve, w=w):
                return ve.reciprocal(out=rcpb[:], in_=pbank[3][0:1, :])
            @S.on("tensor")
            def _(te, w=w):
                return te.matmul(out=pbank[4][0:8, :], lhsT=c_ones1x8[:],
                                 rhs=rcpb[:], start=True, stop=True)
            @S.on("vector")
            def _(ve, w=w):
                return ve.tensor_tensor(out=smwb[:, w * 512:(w + 1) * 512],
                                        in0=expvb[:], in1=pbank[4][0:8, :],
                                        op=OP.mult)
            for ts4 in range(4):
                @S.on("tensor")
                def _(te, w=w, ts4=ts4):
                    return te.transpose(
                        out=pbank[5][:, 0:8],
                        in_=smwb[0:8, w * 512 + ts4 * 128:w * 512 + (ts4 + 1) * 128],
                        identity=c_identf[0:8, 0:8])
                @S.on("vector")
                def _(ve, w=w, ts4=ts4):
                    return ve.tensor_copy(out=smtt[:, 0:8], in_=pbank[5][:, 0:8])
                @S.on("gpsimd")
                def _(g, w=w, ts4=ts4):
                    dma2(g, dram_ap(sm_tm, (w * 512 + ts4 * 128) * 64,
                                    [(64, 128), (1, 64)]), smtt[:])
                    w2(g)

        @S.on("sync")
        def _(sy):
            dma1(sy, sm_half, smwb[:])
            w1(sy)

        _emit_all(S, block, tok)
    return nc




L4_MARKS = []


def build_L4(trunc=None):
    del L4_MARKS[:]
    nc = bass.Bass("TRN2", target_bir_lowering=False, debug=False,
                   detect_race_conditions=False, enable_asserts=False)
    I = lambda n, s, d: nc.dram_tensor(n, s, d, kind="ExternalInput").ap()
    sm_full = I("sm_full", [E, T], F32)
    xln2_st = I("xln2_st", [HALF, D], BF16)
    sm_tm = I("sm_tm", [HALF, 64], F32)
    x1_in = I("x1_in", [HALF, D], F32)
    w1r = I("w1r", [E, NFC, D, 128], F32)
    w2w = I("w2", [E, FF, D], F32)
    b1c = I("b1c", [128, E, NFC], F32)
    b2 = I("b2", [E, D], F32)
    e16 = I("e16", [128, 8], F32)
    e16_t = I("e16_t", [8, 128], F32)
    ltri = I("ltri", [128, 128], F32)
    ones_row_bf = I("ones_row_bf", [1, 128], BF16)
    iota_tok = I("iota_tok", [128, 256], I16)
    neg1_i16 = I("neg1_i16", [128, 256], I16)
    slot_mask = I("slot_mask", [128, 1], U32)
    psi_base = I("psi_base", [8, 1], F32)
    iota_perm = I("iota_perm", [128, 4], F32)
    ones_1x128f = I("ones_1x128f", [1, 128], F32)
    exg_st = nc.dram_tensor("exg_st", [KCAP, D], BF16).ap()

    out_t = nc.dram_tensor("out", [HALF, D], F32, kind="ExternalOutput").ap()
    mask_st = nc.dram_tensor("mask_st", [256, 256], F32).ap()

    sbt = nc.alloc_sbuf_tensor
    c_e16 = sbt("c_e16", [128, 8], F32)
    c_e16t = sbt("c_e16t", [8, 128], F32)
    c_ltri = sbt("c_ltri", [128, 128], F32)
    c_onesrow = sbt("c_onesrow", [1, 128], BF16)
    c_iota = sbt("c_iota", [128, 256], I16)
    c_neg1 = sbt("c_neg1", [128, 256], I16)
    c_slotm = sbt("c_slotm", [128, 1], U32)
    c_b1 = sbt("c_b1", [128, E, NFC], F32)
    c_psib = sbt("c_psib", [8, 1], F32)
    foldb = sbt("foldb", [128, 512], F32)
    cjunk = sbt("cjunk", [128, 512], F32)
    gthf = sbt("gthf", [128, 512], F32)
    gtlf = sbt("gtlf", [128, 512], F32)
    eqf = sbt("eqf", [128, 512], F32)
    cumf = sbt("cumf", [128, 512], F32)
    okf = sbt("okf", [128, 512], F32)
    maskf = sbt("maskf", [128, 512], F32)
    zerf = sbt("zerf", [128, 512], F32)
    maskh = sbt("maskh", [128, 256], F32)
    mask16 = sbt("mask16", [128, 256], I16)
    cand = sbt("cand", [128, 256], I16)
    cand2 = sbt("cand2", [16, E, 256], I16)
    idx16 = sbt("idx16", [128, E, KB16], I16)
    nfound = sbt("nfound", [1, E], U32)
    lo8 = sbt("lo8", [8, 2, 1], F32)
    hi8 = sbt("hi8", [8, 2, 1], F32)
    sum8 = sbt("sum8", [8, 1], F32)
    mid8 = sbt("mid8", [8, 1], F32)
    cnt8 = sbt("cnt8", [8, 1], F32)
    sel8 = sbt("sel8", [8, 1], F32)
    sel8i = sbt("sel8i", [8, 1], I16)
    g8 = sbt("g8", [8, 1], F32)
    midb = sbt("midb", [128, 1], F32)
    lob = sbt("lob", [128, 1], F32)
    hib = sbt("hib", [128, 1], F32)
    needb = sbt("needb", [128, 1], F32)
    rowt = sbt("rowt", [128, 1], F32)
    prefb = sbt("prefb", [128, 1], F32)
    st8 = sbt("st8", [128, 8], F32)
    exb = sbt("exb", [128, 8, KCAP], BF16)
    exm = sbt("exm", [128, 4, D], BF16)
    c_iotap = sbt("c_iotap", [128, 4], F32)
    c_o128f = sbt("c_o128f", [1, 128], F32)
    nf_f = sbt("nf_f", [1, E], F32)
    cntf = sbt("cntf", [128, E], F32)
    idxk = sbt("idxk", [128, 4], I16)
    idxf = sbt("idxf", [128, 4], F32)
    mask4 = sbt("mask4", [128, 4], F32)
    tmpm = sbt("tmpm", [128, 4], F32)
    idxsf = sbt("idxsf", [128, 4], F32)
    idxu = sbt("idxu", [128, 4], U32)
    w1t = sbt("w1t", [128, 2, 8, 128], BF16)
    w2t = sbt("w2t", [128, 2, 1024], BF16)
    h1b = sbt("h1b", [128, NFC, KCAP], BF16)
    gcol = sbt("gcol", [128, 4, 64], F32)
    outt = sbt("outt", [128, 4, 1024], F32)
    b2bf = sbt("b2bf", [1, 2, 1024], BF16)
    pbank = [nc.alloc_psum_tensor(f"pb{i}", [128, 512], F32) for i in range(8)]

    S = Sched()
    dc1 = {"n": 0}
    dc2 = {"n": 0}

    with (
        nc.semaphore("tok") as tok, nc.semaphore("sd1") as sd1,
        nc.semaphore("sd2") as sd2, nc.semaphore("ssc") as ssc,
        nc.Block() as block,
    ):
        def dma1(sy, out, in_):
            sy.dma_start(out=out, in_=in_).then_inc(sd1, 16)
            dc1["n"] += 16

        def dma1t(sy, out, in_):
            sy.dma_start_transpose(out=out, in_=in_).then_inc(sd1, 16)
            dc1["n"] += 16

        def dma2(g, out, in_):
            g.dma_start(out=out, in_=in_).then_inc(sd2, 16)
            dc2["n"] += 16

        def w1(sy):
            sy.wait_ge(sd1, dc1["n"])

        def w2(g):
            g.wait_ge(sd2, dc2["n"])

        @S.on("sync")
        def _(sy):
            for dst, src in [(c_e16, e16), (c_e16t, e16_t), (c_ltri, ltri),
                             (c_onesrow, ones_row_bf), (c_iota, iota_tok),
                             (c_neg1, neg1_i16), (c_slotm, slot_mask),
                             (c_b1, b1c), (c_psib, psi_base),
                             (c_iotap, iota_perm), (c_o128f, ones_1x128f)]:
                dma1(sy, dst[:], src)
            dma1(sy, foldb[:],
                 dram_ap(sm_full, 0, [(T, 8), (512, 16), (1, 512)]))
            dma1(sy, out_t, x1_in)
            w1(sy)

        @S.on("vector")
        def _(ve):
            ve.memset(st8[:], 0)
            ve.memset(zerf[:], 0)
            ve.memset(lo8[:, 0], 0.0)
            ve.memset(hi8[:, 0], 1.0)
            return ve.memset(idx16[:], -1)

        @S.on("scalar")
        def _(sc):
            sc.activation(out=st8[:, 0:1], in_=st8[:, 1:2], func=ACT.Sigmoid)
            return sc.activation(out=st8[:, 0:1], in_=st8[:, 1:2],
                                 func=ACT.Gelu_apprx_tanh)

        @S.on("gpsimd")
        def _(g):
            for _i in range(5):
                g.memset(cjunk[:], 0)
            return g.memset(cjunk[:], 0)

        L4_MARKS.append(("init", len(S.items)))
        for it in range(BISECT):
            p, nx = it % 2, 1 - it % 2
            @S.on("vector")
            def _(ve, p=p):
                ve.tensor_tensor(out=sum8[:], in0=lo8[:, p], in1=hi8[:, p], op=OP.add)
                ve.drain()
                return ve.tensor_scalar_mul(mid8[:], sum8[:], 0.5)
            @S.on("tensor")
            def _(te):
                return te.matmul(out=pbank[6][:, 0:1], lhsT=c_e16t[:],
                                 rhs=mid8[:], start=True, stop=True)
            @S.on("vector")
            def _(ve):
                ve.tensor_copy(out=midb[:], in_=pbank[6][:, 0:1])
                ve.drain()
                return ve.tensor_scalar(out=cjunk[:], in0=foldb[:],
                                        scalar1=midb[:], scalar2=None,
                                        op0=OP.is_gt, op1=OP.add,
                                        accum_out=rowt[:])
            @S.on("tensor")
            def _(te):
                return te.matmul(out=pbank[7][0:8, 0:1], lhsT=c_e16[:],
                                 rhs=rowt[:], start=True, stop=True)
            @S.on("vector")
            def _(ve, p=p, nx=nx):
                ve.tensor_copy(out=cnt8[:], in_=pbank[7][0:8, 0:1])
                ve.drain()
                ve.tensor_scalar(out=sel8[:], in0=cnt8[:], scalar1=512.0,
                                 scalar2=None, op0=OP.is_ge)
                ve.drain()
                ve.tensor_copy(out=sel8i[:], in_=sel8[:])
                ve.drain()
                ve.select(out=lo8[:, nx], mask=sel8i[:], on_true=mid8[:],
                          on_false=lo8[:, p])
                return ve.select(out=hi8[:, nx], mask=sel8i[:], on_true=hi8[:, p],
                                 on_false=mid8[:])

        L4_MARKS.append(("bisect", len(S.items)))
        PF = BISECT % 2
        @S.on("tensor")
        def _(te):
            return te.matmul(out=pbank[6][:, 0:1], lhsT=c_e16t[:],
                             rhs=lo8[:, PF], start=True, stop=True)
        @S.on("vector")
        def _(ve):
            return ve.tensor_copy(out=lob[:], in_=pbank[6][:, 0:1])
        @S.on("tensor")
        def _(te):
            return te.matmul(out=pbank[6][:, 0:1], lhsT=c_e16t[:],
                             rhs=hi8[:, PF], start=True, stop=True)
        @S.on("vector")
        def _(ve):
            return ve.tensor_copy(out=hib[:], in_=pbank[6][:, 0:1])
        @S.on("vector")
        def _(ve):
            ve.tensor_scalar(out=gthf[:], in0=foldb[:], scalar1=hib[:],
                             scalar2=None, op0=OP.is_gt, op1=OP.add,
                             accum_out=rowt[:])
            return ve.tensor_scalar(out=gtlf[:], in0=foldb[:], scalar1=lob[:],
                                    scalar2=None, op0=OP.is_gt)
        @S.on("tensor")
        def _(te):
            return te.matmul(out=pbank[7][0:8, 0:1], lhsT=c_e16[:],
                             rhs=rowt[:], start=True, stop=True)
        @S.on("vector")
        def _(ve):
            ve.tensor_copy(out=g8[:], in_=pbank[7][0:8, 0:1])
            ve.drain()
            return ve.tensor_scalar(out=g8[:], in0=g8[:], scalar1=-1.0,
                                    scalar2=512.0, op0=OP.mult, op1=OP.add)
        @S.on("tensor")
        def _(te):
            return te.matmul(out=pbank[6][:, 0:1], lhsT=c_e16t[:],
                             rhs=g8[:], start=True, stop=True)
        @S.on("vector")
        def _(ve):
            ve.tensor_copy(out=needb[:], in_=pbank[6][:, 0:1])
            ve.drain()
            ve.tensor_tensor(out=eqf[:], in0=gtlf[:], in1=gthf[:],
                             op=OP.subtract)
            ve.drain()
            ve.tensor_scalar(out=cjunk[:], in0=eqf[:], scalar1=1.0,
                             scalar2=None, op0=OP.mult, op1=OP.add,
                             accum_out=rowt[:])
            return ve.tensor_tensor_scan(out=cumf[:], data0=eqf[:],
                                         data1=zerf[:], initial=0.0,
                                         op0=OP.add, op1=OP.add)
        @S.on("tensor")
        def _(te):
            return te.matmul(out=pbank[6][:, 0:1], lhsT=c_ltri[:],
                             rhs=rowt[:], start=True, stop=True)
        @S.on("vector")
        def _(ve):
            ve.tensor_copy(out=prefb[:], in_=pbank[6][:, 0:1])
            ve.drain()
            ve.tensor_scalar(out=cumf[:], in0=cumf[:], scalar1=prefb[:],
                             scalar2=None, op0=OP.add)
            ve.drain()
            ve.tensor_scalar(out=okf[:], in0=cumf[:], scalar1=needb[:],
                             scalar2=None, op0=OP.is_le)
            ve.drain()
            ve.tensor_tensor(out=okf[:], in0=eqf[:], in1=okf[:], op=OP.mult)
            ve.drain()
            return ve.tensor_tensor(out=maskf[:], in0=gthf[:], in1=okf[:],
                                    op=OP.add)
        L4_MARKS.append(("maskf", len(S.items)))
        @S.on("sync")
        def _(sy):
            dma1(sy, mask_st[:, :],
                 maskf[:].rearrange("p (h c) -> p h c", h=2))
            w1(sy)
        @S.on("gpsimd")
        def _(g):
            g.indirect_dma_start(
                out=maskh[:], out_offset=None,
                in_=mask_st[:, :],
                in_offset=IndirectOffsetOnAxis(c_slotm[:], 0),
            ).then_inc(sd2, 16)
            dc2["n"] += 16
            w2(g)
        @S.on("vector")
        def _(ve):
            ve.tensor_copy(out=mask16[:], in_=maskh[:])
            ve.drain()
            ve.select(out=cand[:], mask=mask16[:], on_true=c_iota[:],
                      on_false=c_neg1[:])
            ve.drain()
            return ve.engine_nop()
        @S.on("sync")
        def _(sy):
            for e in range(E):
                dma1(sy, cand2[0:16, e, :], cand[16 * e:16 * (e + 1), :])
            w1(sy)
        L4_MARKS.append(("cand", len(S.items)))
        @S.on("gpsimd")
        def _(g):
            g.load_library(libcfg.sparse_gather)
        for e in range(E):
            @S.on("gpsimd")
            def _(g, e=e):
                return g.sparse_gather(out=idx16[0:16, e, :],
                                       in_=cand2[0:16, e, :],
                                       num_found=nfound[0:1, e:e + 1])
        L4_MARKS.append(("sparse", len(S.items)))

        bc_reg = nc.alloc_register(mybir.EngineType.Pool, "bc_reg")
        @S.on("gpsimd")
        def _(g):
            g.reg_mov(bc_reg, HALF - 1)
        @S.on("vector")
        def _(ve):
            ve.tensor_copy(out=nf_f[:], in_=nfound[:])
            ve.drain()
            return ve.engine_nop()
        @S.on("tensor")
        def _(te):
            return te.matmul(out=pbank[6][:, 0:E], lhsT=c_o128f[:],
                             rhs=nf_f[:], start=True, stop=True)
        @S.on("vector")
        def _(ve):
            ve.tensor_copy(out=cntf[:], in_=pbank[6][:, 0:E])
            ve.drain()
            return ve.engine_nop()
        for e in range(E):
            @S.on("gpsimd")
            def _(g, e=e):
                dma2(g, idxk[:], idx16[0:16, e, :])
                w2(g)
            @S.on("vector")
            def _(ve, e=e):
                ve.tensor_copy(out=idxf[:], in_=idxk[:])
                ve.drain()
                ve.tensor_scalar(out=mask4[:], in0=c_iotap[:],
                                 scalar1=cntf[:, e:e + 1], scalar2=None,
                                 op0=OP.is_lt)
                ve.drain()
                ve.tensor_scalar(out=tmpm[:], in0=mask4[:], scalar1=-65536.0,
                                 scalar2=65536.0, op0=OP.mult, op1=OP.add)
                ve.drain()
                ve.tensor_tensor(out=idxsf[:], in0=tmpm[:], in1=idxf[:],
                                 op=OP.add)
                ve.drain()
                ve.tensor_copy(out=idxu[:], in_=idxsf[:])
                ve.drain()
                return ve.engine_nop()
            @S.on("gpsimd")
            def _(g, e=e):
                for blk in range(4):
                    g.indirect_dma_start(
                        out=exm[:, blk, :], out_offset=None,
                        in_=xln2_st,
                        in_offset=IndirectOffsetOnAxis(idxu[:, blk:blk + 1], 0),
                        bounds_check=bc_reg, oob_is_err=False,
                    ).then_inc(sd2, 16)
                    g.indirect_dma_start(
                        out=gcol[:, blk, :], out_offset=None,
                        in_=sm_tm,
                        in_offset=IndirectOffsetOnAxis(idxu[:, blk:blk + 1], 0),
                        bounds_check=bc_reg, oob_is_err=False,
                    ).then_inc(sd2, 16)
                    dc2["n"] += 32
                w2(g)
                dma2(g, dram_ap(exg_st, 0, [(D, 128), (128 * D, 4), (1, D)]),
                     exm[:])
                g.dma_start(out=b2bf[0:1, e % 2, :], in_=b2[e:e + 1, :]
                            ).then_inc(sd2, 16)
                dc2["n"] += 16
                w2(g)
            @S.on("sync")
            def _(sy, e=e):
                for dc in range(8):
                    dma1t(sy, exb[:, dc, :],
                          dram_ap(exg_st, dc * 128, [(D, KCAP), (1, 128)]))
                w1(sy)
            if e == 0:
                L4_MARKS.append(("e0_gather", len(S.items)))
            for fc in range(NFC):
                @S.on("gpsimd")
                def _(g, e=e, fc=fc):
                    g.dma_start(
                        out=w1t[:, fc % 2],
                        in_=dram_ap(w1r, (e * NFC + fc) * D * 128,
                                    [(128, 128), (128 * 128, 8), (1, 128)]),
                    ).then_inc(sd2, 16)
                    dc2["n"] += 16
                    w2(g)
                @S.on("tensor", weight=8)
                def _(te, e=e, fc=fc):
                    for dcc in range(8):
                        te.matmul(out=pbank[fc % 2][:],
                                  lhsT=w1t[:, fc % 2, dcc, :],
                                  rhs=exb[:, dcc, :], start=(dcc == 0),
                                  stop=(dcc == 7)).then_inc(tok, 1)
                @S.on("scalar")
                def _(sc, e=e, fc=fc):
                    return sc.activation(out=h1b[:, fc, :],
                                         in_=pbank[fc % 2][:],
                                         func=ACT.Gelu_apprx_tanh,
                                         bias=c_b1[:, e, fc:fc + 1])
            if e == 0:
                L4_MARKS.append(("e0_h1", len(S.items)))
            for fc in range(NFC):
                @S.on("gpsimd")
                def _(g, e=e, fc=fc):
                    g.dma_start(
                        out=w2t[:, fc % 2],
                        in_=dram_ap(w2w, (e * FF + fc * 128) * D,
                                    [(D, 128), (1, D)]),
                    ).then_inc(sd2, 16)
                    dc2["n"] += 16
                    w2(g)
                @S.on("tensor", weight=8)
                def _(te, e=e, fc=fc):
                    for tt in range(4):
                        for dh in range(2):
                            te.matmul(out=pbank[tt * 2 + dh][:],
                                      lhsT=h1b[:, fc, tt * 128:(tt + 1) * 128],
                                      rhs=w2t[:, fc % 2, dh * 512:(dh + 1) * 512],
                                      start=(fc == 0), stop=False
                                      ).then_inc(tok, 1)
            @S.on("tensor", weight=8)
            def _(te, e=e):
                for tt in range(4):
                    for dh in range(2):
                        te.matmul(out=pbank[tt * 2 + dh][:],
                                  lhsT=c_onesrow[:],
                                  rhs=b2bf[0:1, e % 2, dh * 512:(dh + 1) * 512],
                                  start=False, stop=True).then_inc(tok, 1)
            for tt in range(4):
                @S.on("vector")
                def _(ve, e=e, tt=tt):
                    for dh in range(2):
                        ve.tensor_scalar(
                            out=outt[:, tt, dh * 512:(dh + 1) * 512],
                            in0=pbank[tt * 2 + dh][:],
                            scalar1=gcol[:, tt, e:e + 1], scalar2=None,
                            op0=OP.mult)
                    ve.drain()
                    return ve.engine_nop()
            @S.on("gpsimd")
            def _(g, e=e):
                for blk in range(4):
                    g.indirect_dma_start(
                        out=out_t,
                        out_offset=IndirectOffsetOnAxis(idxu[:, blk:blk + 1], 0),
                        in_=outt[:, blk, :], in_offset=None,
                        bounds_check=bc_reg, oob_is_err=False,
                        compute_op=OP.add,
                    ).then_inc(ssc, 16)
                g.wait_ge(ssc, 64 * (e + 1))
            L4_MARKS.append((f"e{e}", len(S.items)))

        @S.on("gpsimd")
        def _(g):
            g.wait_ge(ssc, 64 * E)

        if trunc is not None:
            del S.items[trunc:]
        _emit_all(S, block, tok)
    mybir.codegen_inst_isa_subclasses(nc)
    return nc


def _prep_host(inputs):
    x = np.asarray(inputs["L0_attn_x"], np.float32)
    pre = {}
    pre["w_in"] = np.ascontiguousarray(inputs["L0_attn_W_in"], np.float32)
    pre["w_k"] = np.ascontiguousarray(inputs["L0_attn_W_k"], np.float32)
    pre["w_v"] = np.ascontiguousarray(inputs["L0_attn_W_v"], np.float32)
    pre["w_q"] = np.ascontiguousarray(inputs["L0_attn_W_q"], np.float32)
    pre["w_b"] = np.ascontiguousarray(inputs["L0_attn_W_beta"], np.float32)
    pre["b_beta_col"] = np.ascontiguousarray(
        np.asarray(inputs["L0_attn_b_beta"], np.float32).reshape(2, 128).T)
    pre["w_out"] = np.ascontiguousarray(inputs["L0_attn_W_out"], np.float32)
    pre["wg"] = np.ascontiguousarray(inputs["L0_ffn_Wg"], np.float32)
    w1 = np.asarray(inputs["L0_ffn_W1"], np.float32)
    pre["w1r"] = np.ascontiguousarray(
        w1.reshape(E, D, NFC, 128).transpose(0, 2, 1, 3))
    b1 = np.asarray(inputs["L0_ffn_b1"], np.float32)
    pre["b1c"] = np.ascontiguousarray(
        b1.reshape(E, NFC, 128).transpose(2, 0, 1))
    pre["w2"] = np.ascontiguousarray(inputs["L0_ffn_W2"], np.float32)
    pre["b2"] = np.ascontiguousarray(inputs["L0_ffn_b2"], np.float32)
    pre["ident_bf"] = np.eye(128, dtype=ml_dtypes.bfloat16)
    pre["ident_f"] = np.eye(128, dtype=np.float32)
    e4h = np.zeros((128, 4), np.float32)
    for p in range(128):
        e4h[p, p // 32] = 1.0
    pre["e4h"] = e4h
    pre["e4h_t"] = np.ascontiguousarray(e4h.T)
    e16 = np.zeros((128, 8), np.float32)
    for p in range(128):
        e16[p, p // 16] = 1.0
    pre["e16"] = e16
    pre["e16_t"] = np.ascontiguousarray(e16.T)
    lt = np.zeros((128, 128), np.float32)
    for a in range(128):
        for b in range(128):
            if a // 16 == b // 16 and a < b:
                lt[a, b] = 1.0
    pre["ltri"] = lt
    pre["ones_row_bf"] = np.ones((1, 128), ml_dtypes.bfloat16)
    pre["ones8"] = np.ones((8, 1), np.float32)
    pre["ones1x8"] = np.ones((1, 8), np.float32)
    iota = np.zeros((128, 256), np.int16)
    for p in range(128):
        for ccol in range(256):
            iota[p, ccol] = (p % 16) * 256 + ccol
    pre["iota_tok"] = iota
    pre["neg1_i16"] = np.full((128, 256), -1, np.int16)
    cb = np.zeros((128, 4), np.float32)
    cb[:, 0] = EPS_LN
    cb[:, 1] = EPS_KN
    pre["cbias"] = cb
    per_core = []
    for c in range(NCORE):
        b, psi = c // 2, c % 2
        xp = np.zeros((NTOK, D), np.float32)
        lo = psi * HALF - W
        src_lo = max(lo, 0)
        xp[src_lo - lo:, :] = x[b, src_lo:psi * HALF + HALF, :]
        m = dict(pre)
        m["x_pad"] = xp
        sm = np.zeros((128, 1), np.uint32)
        for p in range(128):
            e, p2 = p // 16, p % 16
            sm[p, 0] = e * 32 + psi * 16 + p2
        m["slot_mask"] = sm
        m["psi_base"] = np.full((8, 1), float(psi * HALF), np.float32)
        iop = np.zeros((128, 4), np.float32)
        for p in range(128):
            for cc in range(4):
                l = 4 * p + cc
                iop[p, cc] = float(l // 32 + 16 * (l % 32))
        m["iota_perm"] = iop
        m["ones_1x128f"] = np.ones((1, 128), np.float32)
        per_core.append(m)
    return per_core


def _run4(per_core, runner):
    n = len(per_core)
    nc1 = build_L1()
    r1 = runner(nc1, [{k: m[k] for k in
                       ("x_pad", "w_in", "w_k", "w_v", "w_q", "w_b",
                        "b_beta_col", "ident_bf", "ident_f", "e4h", "e4h_t",
                        "cbias")} for m in per_core])
    nc2 = build_L2()
    r2 = runner(nc2, [{k: r1[i][k] for k in
                       ("kn_st", "q_st", "b_st", "v_st")} for i in range(n)])
    nc3 = build_L3()
    r3 = runner(nc3, [dict(outs_st=r2[i]["outs_st"], x_pad=per_core[i]["x_pad"],
                           w_out=per_core[i]["w_out"], wg=per_core[i]["wg"],
                           ident_bf=per_core[i]["ident_bf"],
                           ident_f=per_core[i]["ident_f"],
                           ones8=per_core[i]["ones8"],
                           ones1x8=per_core[i]["ones1x8"],
                           cbias=per_core[i]["cbias"]) for i in range(n)])
    sm_fulls = []
    for i in range(n):
        pair = i ^ 1
        a = r3[i]["sm_half"]
        bb = r3[pair]["sm_half"] if pair < n else np.zeros_like(a)
        halves = (a, bb) if i % 2 == 0 else (bb, a)
        sm_fulls.append(np.concatenate(halves, axis=1))
    nc4 = build_L4()
    r4 = runner(nc4, [dict(sm_full=sm_fulls[i], xln2_st=r3[i]["xln2_st"],
                           sm_tm=r3[i]["sm_tm"], x1_in=r3[i]["x1_out"],
                           w1r=per_core[i]["w1r"], w2=per_core[i]["w2"],
                           b1c=per_core[i]["b1c"], b2=per_core[i]["b2"],
                           e16=per_core[i]["e16"], e16_t=per_core[i]["e16_t"],
                           ltri=per_core[i]["ltri"],
                           ones_row_bf=per_core[i]["ones_row_bf"],
                           iota_tok=per_core[i]["iota_tok"],
                           neg1_i16=per_core[i]["neg1_i16"],
                           slot_mask=per_core[i]["slot_mask"],
                           psi_base=per_core[i]["psi_base"],
                           iota_perm=per_core[i]["iota_perm"],
                           ones_1x128f=per_core[i]["ones_1x128f"])
                      for i in range(n)])
    return r4


def kernel(**inputs):
    per_core = _prep_host(inputs)

    def runner(nc, in_maps):
        res = run_bass_kernel_spmd(nc, in_maps, core_ids=list(range(len(in_maps))))
        return res.results

    r4 = _run4(per_core, runner)
    out = np.zeros((B, T, D), np.float32)
    for c in range(NCORE):
        b, psi = c // 2, c % 2
        out[b, psi * HALF:(psi + 1) * HALF, :] = r4[c]["out"]
    return out
